# version 15
# baseline (speedup 1.0000x reference)
import math
from contextlib import ExitStack

import numpy as np

import concourse.bass as bass
import concourse.mybir as mybir
from concourse.bass_utils import run_bass_kernel_spmd

F32 = mybir.dt.float32
BF16 = mybir.dt.bfloat16
AF = mybir.ActivationFunctionType
ALU = mybir.AluOpType
AX = mybir.AxisListType

NCORES = 8
T = 1024
D = 4096
H = 32
S = 8192
FO = 3 * D
NT = T // 128
NF = D // 128
SM_SCALE = 1.0 / math.sqrt(128.0)

MACS_K = list(range(8, 16))
MACS_V = list(range(16, 24))
MACS_Q = list(range(0, 8))
MAC_ORDER = MACS_Q + MACS_K + MACS_V


class _FakeInstr:
    def then_inc(self, *a, **k):
        return self


class _FakeEng:
    def __getattr__(self, name):
        return lambda *a, **k: _FakeInstr()


class Ctx:

    def __init__(self):
        self.mode = "dry"
        self.ev = {}
        self.marks = {}
        self.dry = {}
        self.emitc = {}

    def inc(self, instr, sem, k, key=None):
        cnt = self.dry if self.mode == "dry" else self.emitc
        name = sem[0]
        cnt[name] = cnt.get(name, 0) + k
        if self.mode == "dry":
            if key is not None:
                assert key not in self.ev, key
                self.ev[key] = (sem, cnt[name])
        else:
            if key is not None:
                s, v = self.ev[key]
                assert s[0] == name and v == cnt[name], (key, v, cnt[name])
            instr.then_inc(sem[1], k)
        return cnt[name]

    def wait(self, eng, key):
        if self.mode == "dry":
            return
        sem, val = self.ev[key]
        eng.wait_ge(sem[1], val)

    def wait_val(self, eng, sem, val):
        if self.mode == "dry":
            return
        eng.wait_ge(sem[1], val)

    def mark(self, key, sems):
        cnt = self.dry if self.mode == "dry" else self.emitc
        snap = [(s, cnt.get(s[0], 0)) for s in sems]
        if self.mode == "dry":
            self.marks[key] = snap
        else:
            assert self.marks[key] == snap, (key, self.marks[key], snap)

    def wait_mark(self, eng, key):
        if self.mode == "dry":
            return
        for sem, val in self.marks[key]:
            if val > 0:
                eng.wait_ge(sem[1], val)


def build(dbg=False):
    nc = bass.Bass("TRN2", target_bir_lowering=False)

    x_in = nc.dram_tensor("x", [T, D], F32, kind="ExternalInput").ap()
    wqkv = nc.dram_tensor("wqkv", [D, FO], F32, kind="ExternalInput").ap()
    bqkv = nc.dram_tensor("bqkv", [FO], F32, kind="ExternalInput").ap()
    wo = nc.dram_tensor("wo", [D, D], F32, kind="ExternalInput").ap()
    ob_in = nc.dram_tensor("ob", [D], F32, kind="ExternalInput").ap()
    wt_in = nc.dram_tensor("wt", [D, D], F32, kind="ExternalInput").ap()
    screp = nc.dram_tensor("screp", [128, 1024], F32, kind="ExternalInput").ap()

    y_out = nc.dram_tensor("y", [T, D], F32, kind="ExternalOutput").ap()

    k_src = nc.dram_tensor("k_src", [D, T], BF16, kind="Internal").ap()
    v_src = nc.dram_tensor("v_src", [D, T], BF16, kind="Internal").ap()
    gath_k = [nc.dram_tensor(f"gath_k{i}", [NCORES * D // 4, T], BF16, kind="Internal").ap() for i in range(4)]
    gath_v = [nc.dram_tensor(f"gath_v{i}", [NCORES * D // 4, T], BF16, kind="Internal").ap() for i in range(4)]
    q_dram = nc.dram_tensor("q_dram", [D, T], BF16, kind="Internal").ap()
    g_dram = nc.dram_tensor("g_dram", [T, D], BF16, kind="Internal").ap()

    v_src_4d = v_src.rearrange("(hh p) (tt dd) -> p hh tt dd", p=128, dd=128)

    C = Ctx()

    with ExitStack() as top:
        def sem(name):
            return (name, top.enter_context(nc.semaphore(name)))

        def sems(name, n):
            return [sem(f"{name}{i}") for i in range(n)]

        s_pe = sem("s_pe")
        s_ac = sem("s_ac")
        s_dv = sem("s_dv")
        s_gc = sem("s_gc")
        s_colk = sems("s_colk", 4)
        s_colv = sems("s_colv", 4)
        s_x = sems("s_x", 2)
        s_bias = sem("s_bias")
        s_scl = sem("s_scl")
        s_wq = sems("s_wq", 3)
        s_st = sems("s_st", 4)
        s_wt = sems("s_wt", 3)
        s_gs = sems("s_gs", 4)
        s_kv = sems("s_kv", 2)
        s_wo = sems("s_wo", 3)
        s_cx = sems("s_cx", 3)
        s_y = sems("s_y", 3)

        ones_bf = top.enter_context(nc.sbuf_tensor("ones_bf", [1, 512], BF16))
        ones_f = top.enter_context(nc.sbuf_tensor("ones_f", [128, 1], F32))
        ones_fr = top.enter_context(nc.sbuf_tensor("ones_fr", [1, 128], F32))
        epst = top.enter_context(nc.sbuf_tensor("epst", [128, 1], F32))
        ident = top.enter_context(nc.sbuf_tensor("ident", [128, 128], BF16))
        ob_bf = top.enter_context(nc.sbuf_tensor("ob_bf", [1, D], BF16))

        def ph1_sync(e):
            for tt in range(NT):
                if tt >= 2:
                    C.wait(e, ("xn", tt - 2))
                C.inc(e.dma_start(out=P1.xbuf[tt % 2][:],
                                  in_=x_in[128 * tt:128 * (tt + 1), :]),
                      s_x[tt % 2], 16, ("xld", tt))

        def ph1_gp(e):
            C.inc(e.memset(ident[:], 0.0), s_gc, 1, ("identm",))
            C.wait(e, ("identm",))
            C.inc(e.affine_select(out=ident[:], in_=ident[:],
                                  compare_op=ALU.not_equal, fill=1.0, base=0,
                                  pattern=[[-1, 128]], channel_multiplier=1),
                  s_gc, 1, ("ident",))
            C.inc(e.dma_start(out=A.bq_bf[:], in_=bqkv[None, :]), s_bias, 16)
            C.inc(e.dma_start(out=ob_bf[:], in_=ob_in[None, :]), s_bias, 16,
                  ("biasld",))

        def ph1_act(e):
            for tt in range(NT):
                C.wait(e, ("xld", tt))
                C.inc(e.activation(P1.sq_dump[:], P1.xbuf[tt % 2][:], AF.Square,
                                   accum_out=P1.ssq[tt % 2][:]),
                      s_ac, 1, ("ssq", tt))
                C.wait(e, ("var", tt))
                C.inc(e.activation(P1.lnv[tt % 2][:], P1.varb[tt % 2][:], AF.Ln,
                                   bias=epst[:]), s_ac, 1, ("lnv", tt))
                C.wait(e, ("lnv", tt))
                C.inc(e.activation(P1.rstd[tt % 2][:], P1.lnv[tt % 2][:], AF.Exp,
                                   scale=-0.5), s_ac, 1, ("rstd", tt))

        def ph1_dve(e):
            C.inc(e.memset(ones_bf[:], 1.0), s_dv, 1)
            C.inc(e.memset(ones_f[:], 1.0), s_dv, 1)
            C.inc(e.memset(ones_fr[:], 1.0), s_dv, 1)
            C.inc(e.memset(epst[:], 1e-5), s_dv, 1, ("consts",))
            for tt in range(NT):
                C.wait(e, ("xld", tt))
                C.inc(e.reduce_sum(P1.srow[tt % 2][:], P1.xbuf[tt % 2][:], axis=AX.X),
                      s_dv, 1, ("srow", tt))
                C.wait(e, ("srow", tt))
                C.inc(e.tensor_scalar_mul(P1.mean[tt % 2][:], P1.srow[tt % 2][:], 1.0 / D),
                      s_dv, 1, ("mean", tt))
                C.wait(e, ("ssq", tt))
                C.inc(e.tensor_scalar_mul(P1.vtmp[tt % 2][:], P1.ssq[tt % 2][:], 1.0 / D),
                      s_dv, 1, ("vtmp", tt))
                C.wait(e, ("mean", tt))
                C.inc(e.tensor_tensor(P1.m2[tt % 2][:], P1.mean[tt % 2][:],
                                      P1.mean[tt % 2][:], ALU.mult),
                      s_dv, 1, ("m2", tt))
                C.wait(e, ("m2", tt))
                C.inc(e.tensor_tensor(P1.varb[tt % 2][:], P1.vtmp[tt % 2][:],
                                      P1.m2[tt % 2][:], ALU.subtract),
                      s_dv, 1, ("var", tt))
                C.wait(e, ("rstd", tt))
                if tt >= 2:
                    C.wait(e, ("tr", tt - 2, NF - 1))
                C.inc(e.tensor_scalar(out=P1.xn_t[tt % 2][:], in0=P1.xbuf[tt % 2][:],
                                      scalar1=P1.mean[tt % 2][:],
                                      scalar2=P1.rstd[tt % 2][:],
                                      op0=ALU.subtract, op1=ALU.mult),
                      s_dv, 1, ("xn", tt))
                for fb in range(NF):
                    C.wait(e, ("tr", tt, fb))
                    C.inc(e.tensor_copy(A.xnT[:, fb * T + tt * 128: fb * T + (tt + 1) * 128],
                                        P1.pst[fb % 2][:]),
                          s_dv, 1, ("tre", tt, fb))

        def ph1_pe(e):
            for tt in range(NT):
                C.wait(e, ("xn", tt))
                if tt == 0:
                    C.wait(e, ("ident",))
                for fb in range(NF):
                    if tt > 0 or fb >= 2:
                        pi = tt * NF + fb - 2
                        C.wait(e, ("tre", pi // NF, pi % NF))
                    C.inc(e.transpose(P1.pst[fb % 2][:],
                                      P1.xn_t[tt % 2][:, 128 * fb:128 * (fb + 1)],
                                      ident[:]),
                          s_pe, 1, ("tr", tt, fb))

        def qkv_dst(m, j, th):
            if m >= 8:
                r0 = 512 * (m - 8) + 128 * j
                return k_src[r0:r0 + 128, 512 * th:512 * (th + 1)]
            r0 = 512 * m + 128 * j
            return q_dram[r0:r0 + 128, 512 * th:512 * (th + 1)]

        def ph3_gp(e):
            for q in range(4):
                if q == 0:
                    C.wait(e, ("mmf", 4 * NF - 1))
                elif q == 1:
                    C.wait(e, ("mmf", 7 * NF - 1))
                elif q == 2:
                    C.wait(e, ("mmf", 8 * NF - 1))
                C.wait_mark(e, f"k_st{q}")
                C.inc(e.collective_compute(
                    "AllGather", ALU.bypass,
                    replica_groups=[list(range(NCORES))],
                    ins=[k_src[1024 * q:1024 * (q + 1), :].opt()],
                    outs=[gath_k[q].opt()]),
                    s_colk[q], 1, ("colk", q))
                C.wait_mark(e, f"v_st{q}")
                C.inc(e.collective_compute(
                    "AllGather", ALU.bypass,
                    replica_groups=[list(range(NCORES))],
                    ins=[v_src[1024 * q:1024 * (q + 1), :].opt()],
                    outs=[gath_v[q].opt()]),
                    s_colv[q], 1, ("colv", q))

        def ph2_scalar(e):
            C.wait(e, ("tre", NT - 1, NF - 1))
            for mi, m in enumerate(MAC_ORDER):
                for fb in range(NF):
                    n = mi * NF + fb
                    if n >= 3:
                        C.wait(e, ("wqc", n - 3))
                    C.inc(e.dma_start(out=P2.wstg[n % 3][:],
                                      in_=wqkv[128 * fb:128 * (fb + 1),
                                               512 * m:512 * (m + 1)]),
                          s_wq[n % 3], 16, ("wq", n))

        def ph2_pe(e):
            for mi, m in enumerate(MAC_ORDER):
                std = m < 16
                for fb in range(NF):
                    n = mi * NF + fb
                    C.wait(e, ("wqc", n))
                    if fb == 0:
                        if mi == 0:
                            C.wait(e, ("consts",))
                            C.wait(e, ("biasld",))
                    for k in range(8):
                        if fb == 0 and mi >= 1:
                            C.wait(e, ("evq", mi - 1, k))
                        if std:
                            j, th = k // 2, k % 2
                            ins = e.matmul(P2.ps[k][:],
                                           P2.wqbuf[n % 3][:, 128 * j:128 * (j + 1)],
                                           A.xnT[:, fb * T + 512 * th: fb * T + 512 * (th + 1)],
                                           start=(fb == 0), stop=False)
                        else:
                            tt = k
                            ins = e.matmul(P2.ps[k][:],
                                           A.xnT[:, fb * T + 128 * tt: fb * T + 128 * (tt + 1)],
                                           P2.wqbuf[n % 3][:, 0:512],
                                           start=(fb == 0), stop=False)
                        if k == 7:
                            C.inc(ins, s_pe, 1, ("mmq", n))
                for k in range(8):
                    if std:
                        j, th = k // 2, k % 2
                        ins = e.matmul(P2.ps[k][:],
                                       A.bq_bf[0:1, 512 * m + 128 * j: 512 * m + 128 * (j + 1)],
                                       ones_bf[0:1, 0:512], start=False, stop=True)
                    else:
                        ins = e.matmul(P2.ps[k][:],
                                       ones_bf[0:1, 0:128],
                                       A.bq_bf[0:1, 512 * m:512 * (m + 1)],
                                       start=False, stop=True)
                    if k == 7:
                        C.inc(ins, s_pe, 1, ("pst", mi))

        def ph2_dve(e):
            def cast(n):
                C.wait(e, ("wq", n))
                if n >= 3:
                    C.wait(e, ("mmq", n - 3))
                C.inc(e.tensor_copy(P2.wqbuf[n % 3][:], P2.wstg[n % 3][:]),
                      s_dv, 1, ("wqc", n))

            def evacs(mi):
                C.wait(e, ("pst", mi))
                for k in range(8):
                    si = mi * 8 + k
                    if si >= 4:
                        C.wait(e, ("kvst", (si - 4) // 8, (si - 4) % 8))
                    C.inc(e.tensor_copy(P2.stage[si % 4][:], P2.ps[k][:]),
                          s_dv, 1, ("evq", mi, k))

            for mi in range(len(MAC_ORDER)):
                for fb in range(3):
                    cast(mi * NF + fb)
                if mi >= 1:
                    evacs(mi - 1)
                for fb in range(3, NF):
                    cast(mi * NF + fb)
            evacs(len(MAC_ORDER) - 1)

        def ph2_sync(e):
            for mi, m in enumerate(MAC_ORDER):
                for k in range(8):
                    si = mi * 8 + k
                    C.wait(e, ("evq", mi, k))
                    if m < 16:
                        j, th = k // 2, k % 2
                        dst = qkv_dst(m, j, th)
                        src = P2.stage[si % 4][:]
                    else:
                        tt = k
                        h0 = 4 * (m - 16)
                        dst = v_src_4d[:, h0:h0 + 4, tt, :]
                        src = P2.stage[si % 4][:].rearrange("p (hs dd) -> p hs dd", dd=128)
                    C.inc(e.dma_start(out=dst, in_=src), s_st[si % 4], 16,
                          ("kvst", mi, k))
                if mi == 7:
                    C.mark("q_stored", s_st)
                if mi >= 9 and mi <= 15 and mi % 2 == 1:
                    C.mark(f"k_st{(mi - 9) // 2}", s_st)
                if mi >= 17 and mi % 2 == 1:
                    C.mark(f"v_st{(mi - 17) // 2}", s_st)

        def ph3_sync(e):
            C.wait_mark(e, "v_st3")
            C.inc(e.dma_start(out=P3.sc_all[:], in_=screp), s_scl, 16, ("scld",))
            for oc in range(8):
                for ib in range(NF):
                    n = oc * NF + ib
                    if n >= 3:
                        C.wait(e, ("dq", n - 3))
                    C.inc(e.dma_start(out=P3.wtbuf[n % 3][:],
                                      in_=wt_in[128 * ib:128 * (ib + 1),
                                                512 * oc:512 * (oc + 1)]),
                          s_wt[n % 3], 16, ("wt", n))
                for tt in range(NT):
                    gi = oc * NT + tt
                    C.wait(e, ("sil", oc, tt))
                    C.inc(e.dma_start(out=g_dram[128 * tt:128 * (tt + 1),
                                                 512 * oc:512 * (oc + 1)],
                                      in_=P3.g_stage[gi % 4][:]),
                          s_gs[gi % 4], 16, ("gst", oc, tt))
                if oc == 7:
                    C.mark("g_stored", s_gs)

        def ph3_dve(e):
            for oc in range(8):
                for ib in range(NF):
                    n = oc * NF + ib
                    C.wait(e, ("wt", n))
                    if n == 0:
                        C.wait(e, ("scld",))
                    if n >= 2:
                        C.wait(e, ("mmf", n - 2))
                    for c in range(4):
                        o_blk = 4 * oc + c
                        ins = e.tensor_scalar_mul(
                            P3.wdq[n % 2][:, 128 * c:128 * (c + 1)],
                            P3.wtbuf[n % 3][:, 128 * c:128 * (c + 1)],
                            P3.sc_all[:, 32 * o_blk + ib:32 * o_blk + ib + 1])
                        if c == 3:
                            C.inc(ins, s_dv, 1, ("dq", n))

        def ph3_pe(e):
            for oc in range(8):
                for ib in range(NF):
                    n = oc * NF + ib
                    C.wait(e, ("dq", n))
                    for tt in range(NT):
                        if ib == 0 and oc >= 1:
                            C.wait(e, ("sil", oc - 1, tt))
                        ins = e.matmul(P3.ps[tt][:],
                                       A.xnT[:, ib * T + 128 * tt: ib * T + 128 * (tt + 1)],
                                       P3.wdq[n % 2][:, 0:512],
                                       start=(ib == 0), stop=(ib == NF - 1))
                        if tt == NT - 1:
                            C.inc(ins, s_pe, 1, ("mmf", n))

        def ph3_act(e):
            for oc in range(8):
                C.wait(e, ("mmf", oc * NF + NF - 1))
                for tt in range(NT):
                    gi = oc * NT + tt
                    if gi >= 4:
                        C.wait(e, ("gst", (gi - 4) // NT, (gi - 4) % NT))
                    C.inc(e.activation(P3.g_stage[gi % 4][:], P3.ps[tt][:], AF.Silu),
                          s_ac, 1, ("sil", oc, tt))

        def ph4_sync(e):
            for h in range(H):
                if h == 0:
                    C.wait_mark(e, "v_st3")
                    C.wait_mark(e, "g_stored")
                    C.wait(e, ("mmf", 8 * NF - 1))
                    C.wait(e, ("sil", 7, NT - 1))
                if h % 8 == 0:
                    C.wait_val(e, s_colk[h // 8], 1)
                    C.wait_val(e, s_colv[h // 8], 1)
                if h >= 2:
                    C.wait(e, ("ctxe", h - 2))
                qq, hh = h // 8, h % 8
                for r in range(NCORES):
                    C.inc(e.dma_start(
                        out=P4.kbuf[h % 2][:, T * r:T * (r + 1)],
                        in_=gath_k[qq][1024 * r + 128 * hh: 1024 * r + 128 * (hh + 1), :]),
                        s_kv[h % 2], 16)
                for r in range(NCORES):
                    C.inc(e.dma_start(
                        out=P4.vbuf[h % 2][:, T * r:T * (r + 1)],
                        in_=gath_v[qq][1024 * r + 128 * hh: 1024 * r + 128 * (hh + 1), :]),
                        s_kv[h % 2], 16)
                C.inc(e.dma_start(out=P4.qbuf[h % 2][:],
                                  in_=q_dram[128 * h:128 * (h + 1), :]),
                      s_kv[h % 2], 16, ("kvld", h))

        def ph4_pe(e):
            for h in range(H):
                C.wait(e, ("kvld", h))

                def smm(kc):
                    g = h * 64 + kc
                    if g >= 2:
                        C.wait(e, ("exp", g - 2))
                    if kc == 0 and h >= 1:
                        C.wait(e, ("rcp", h - 1))
                    if kc == 1 and h >= 1:
                        C.wait(e, ("rlbc", h - 1))
                    for th in range(2):
                        ins = e.matmul(P4.ps_s[kc % 2][:, 512 * th:512 * (th + 1)],
                                       P4.kbuf[h % 2][:, 128 * kc:128 * (kc + 1)],
                                       P4.qbuf[h % 2][:, 512 * th:512 * (th + 1)],
                                       start=True, stop=True)
                        if th == 1:
                            C.inc(ins, s_pe, 1, ("smm", g))

                def cmm(kc):
                    g = h * 64 + kc
                    C.wait(e, ("exp", g))
                    if kc == 0 and h >= 1:
                        C.wait(e, ("ctxe", h - 1))
                    for th in range(2):
                        ins = e.matmul(P4.ps_c[:, 512 * th:512 * (th + 1)],
                                       P4.vbuf[h % 2][:, 128 * kc:128 * (kc + 1)],
                                       P4.pbuf[g % NPB][:, 512 * th:512 * (th + 1)],
                                       start=(kc == 0), stop=(kc == 63))
                        if th == 1:
                            C.inc(ins, s_pe, 1, ("cmm", g))

                for kc in range(64):
                    smm(kc)
                    if kc >= 1:
                        cmm(kc - 1)
                cmm(63)
                C.wait(e, ("radd", h * 64 + 63))
                C.wait(e, ("radg", h * 64 + 63))
                for th in range(2):
                    ins = e.matmul(P4.ps_s[0][0:1, 512 * th:512 * (th + 1)],
                                   ones_f[:], P4.R[:, 512 * th:512 * (th + 1)],
                                   start=True, stop=True)
                    if th == 1:
                        C.inc(ins, s_pe, 1, ("lmm", h))
                C.wait(e, ("rcp", h))
                for th in range(2):
                    ins = e.matmul(P4.ps_s[1][:, 512 * th:512 * (th + 1)],
                                   ones_fr[:], P4.rl_row[0:1, 512 * th:512 * (th + 1)],
                                   start=True, stop=True)
                    if th == 1:
                        C.inc(ins, s_pe, 1, ("rlb", h))

        DV_W = 736
        NPB = 6

        def ph4_act(e):
            for h in range(H):
                for kc in range(64):
                    g = h * 64 + kc
                    C.wait(e, ("smm", g))
                    if g >= NPB:
                        C.wait(e, ("cmm", g - NPB))
                        C.wait(e, ("radd", g - NPB))
                        C.wait(e, ("radg", g - NPB))
                    C.inc(e.activation(P4.pbuf[g % NPB][:], P4.ps_s[kc % 2][:],
                                       AF.Exp, scale=SM_SCALE),
                          s_ac, 1, ("exp", g))
                C.wait(e, ("lmm", h))
                if h >= 1:
                    C.wait(e, ("rlb", h - 1))
                C.inc(e.activation(P4.lntmp[:], P4.ps_s[0][0:1, 0:1024], AF.Ln),
                      s_ac, 1, ("lnl", h))
                C.wait(e, ("lnl", h))
                C.inc(e.activation(P4.rl_row[:], P4.lntmp[:], AF.Exp, scale=-1.0),
                      s_ac, 1, ("rcp", h))

        def ph4_gp(e):
            for h in range(H):
                for kc in range(64):
                    g = h * 64 + kc
                    C.wait(e, ("exp", g))
                    if kc == 0:
                        if h >= 1:
                            C.wait(e, ("lmm", h - 1))
                        ins = e.tensor_copy(P4.R[:, DV_W:T], P4.pbuf[g % NPB][:, DV_W:T])
                    else:
                        C.wait(e, ("radg", g - 1))
                        ins = e.tensor_tensor(P4.R[:, DV_W:T], P4.R[:, DV_W:T],
                                              P4.pbuf[g % NPB][:, DV_W:T], ALU.add)
                    C.inc(ins, s_gc, 1, ("radg", g))

        def ph4_dve(e):
            for h in range(H):
                for kc in range(64):
                    g = h * 64 + kc
                    C.wait(e, ("exp", g))
                    if kc == 0:
                        ins = e.tensor_copy(P4.R[:, 0:DV_W], P4.pbuf[g % NPB][:, 0:DV_W])
                    else:
                        C.wait(e, ("radd", g - 1))
                        ins = e.tensor_tensor(P4.R[:, 0:DV_W], P4.R[:, 0:DV_W],
                                              P4.pbuf[g % NPB][:, 0:DV_W], ALU.add)
                    C.inc(ins, s_dv, 1, ("radd", g))
                C.wait(e, ("rlb", h))
                C.inc(e.tensor_copy(P4.rlb_sb[:], P4.ps_s[1][:]), s_dv, 1, ("rlbc", h))
                C.wait(e, ("rlbc", h))
                C.wait(e, ("cmm", h * 64 + 63))
                C.inc(e.tensor_tensor(A.ctxT[:, T * h:T * (h + 1)], P4.ps_c[:],
                                      P4.rlb_sb[:], ALU.mult),
                      s_dv, 1, ("ctxe", h))

        def ph5_scalar(e):
            C.wait(e, ("ctxe", H - 1))
            for oc in range(8):
                for fb in range(NF):
                    n = oc * NF + fb
                    if n >= 3:
                        C.wait(e, ("woc", n - 3))
                    C.inc(e.dma_start(out=P5.wstg[n % 3][:],
                                      in_=wo[128 * fb:128 * (fb + 1),
                                             512 * oc:512 * (oc + 1)]),
                          s_wo[n % 3], 16, ("wo", n))

        def ph5_pe(e):
            for oc in range(8):
                for fb in range(NF):
                    n = oc * NF + fb
                    C.wait(e, ("woc", n))
                    for qt in range(NT):
                        if fb == 0 and oc >= 1:
                            C.wait(e, ("cmb", (oc - 1) * 8 + qt))
                        ins = e.matmul(P5.ps[qt][:],
                                       A.ctxT[:, fb * T + 128 * qt: fb * T + 128 * (qt + 1)],
                                       P5.owbuf[n % 3][:, 0:512],
                                       start=(fb == 0), stop=False)
                        if qt == NT - 1:
                            C.inc(ins, s_pe, 1, ("mmo", n))
                for qt in range(NT):
                    ins = e.matmul(P5.ps[qt][:], ones_bf[0:1, 0:128],
                                   ob_bf[0:1, 512 * oc:512 * (oc + 1)],
                                   start=False, stop=True)
                    if qt == NT - 1:
                        C.inc(ins, s_pe, 1, ("obias", oc))

        def ph5_sync(e):
            def load(ci):
                oc, qt = ci // 8, ci % 8
                C.inc(e.dma_start(out=P5.xtile[ci % 3][:],
                                  in_=x_in[128 * qt:128 * (qt + 1),
                                           512 * oc:512 * (oc + 1)]),
                      s_cx[ci % 3], 16)
                C.inc(e.dma_start(out=P5.gtile[ci % 3][:],
                                  in_=g_dram[128 * qt:128 * (qt + 1),
                                             512 * oc:512 * (oc + 1)]),
                      s_cx[ci % 3], 16, ("cg", ci))

            C.wait(e, ("ctxe", H - 1))
            for ci in range(3):
                load(ci)
            for ci in range(64):
                oc, qt = ci // 8, ci % 8
                C.wait(e, ("cmb", ci))
                C.inc(e.dma_start(out=y_out[128 * qt:128 * (qt + 1),
                                            512 * oc:512 * (oc + 1)],
                                  in_=P5.ostage[ci % 3][:]),
                      s_y[ci % 3], 16, ("yst", ci))
                if ci + 3 < 64:
                    load(ci + 3)
            C.mark("y_done", s_y)
            C.wait_mark(e, "y_done")

        def ph5_dve(e):
            def cast(n):
                C.wait(e, ("wo", n))
                if n >= 3:
                    C.wait(e, ("mmo", n - 3))
                C.inc(e.tensor_copy(P5.owbuf[n % 3][:], P5.wstg[n % 3][:]),
                      s_dv, 1, ("woc", n))

            def combine(ci):
                oc, qt = ci // 8, ci % 8
                C.wait(e, ("obias", oc))
                C.wait(e, ("cg", ci))
                if ci >= 2:
                    C.wait(e, ("cmb", ci - 2))
                C.inc(e.tensor_tensor(P5.u[ci % 2][:], P5.ps[qt][:],
                                      P5.gtile[ci % 3][:], ALU.mult),
                      s_dv, 1, ("umul", ci))
                C.wait(e, ("umul", ci))
                if ci >= 3:
                    C.wait(e, ("yst", ci - 3))
                C.inc(e.tensor_tensor(P5.ostage[ci % 3][:], P5.u[ci % 2][:],
                                      P5.xtile[ci % 3][:], ALU.add),
                      s_dv, 1, ("cmb", ci))

            for oc in range(8):
                for fb in range(3):
                    cast(oc * NF + fb)
                if oc >= 1:
                    for k in range(8):
                        combine((oc - 1) * 8 + k)
                for fb in range(3, NF):
                    cast(oc * NF + fb)
            for k in range(8):
                combine(7 * 8 + k)

        class NS:
            pass

        A = NS()
        P1 = NS()
        P2 = NS()
        P3 = NS()
        P4 = NS()
        P5 = NS()

        def run_phase(block, emitters):
            C.mode = "dry"
            for _, fn in emitters:
                fn(_FakeEng())
            C.mode = "emit"
            for sec, fn in emitters:
                getattr(block, sec)(lambda e, fn=fn: fn(e))

        with ExitStack() as mid:
            A.xnT = mid.enter_context(nc.sbuf_tensor("xnT", [128, NF * T], BF16))
            A.bq_bf = mid.enter_context(nc.sbuf_tensor("bq_bf", [1, FO], BF16))

            with ExitStack() as ph:
                P1.xbuf = [ph.enter_context(nc.sbuf_tensor(f"xbuf{i}", [128, D], F32)) for i in range(2)]
                P1.xn_t = [ph.enter_context(nc.sbuf_tensor(f"xn_t{i}", [128, D], BF16)) for i in range(2)]
                P1.sq_dump = ph.enter_context(nc.sbuf_tensor("sq_dump", [128, D], BF16))
                for nm in ["srow", "ssq", "mean", "vtmp", "m2", "varb", "lnv", "rstd"]:
                    setattr(P1, nm, [ph.enter_context(nc.sbuf_tensor(f"{nm}{i}", [128, 1], F32)) for i in range(2)])
                P1.pst = [ph.enter_context(nc.psum_tensor(f"pst{i}", [128, 128], BF16)) for i in range(2)]
                block = ph.enter_context(nc.Block("ph1"))
                run_phase(block, [("sync", ph1_sync), ("gpsimd", ph1_gp),
                                  ("scalar", ph1_act), ("vector", ph1_dve),
                                  ("tensor", ph1_pe)])

            with ExitStack() as ph:
                P2.wstg = [ph.enter_context(nc.sbuf_tensor(f"wstg{i}", [128, 512], F32)) for i in range(3)]
                P2.wqbuf = [ph.enter_context(nc.sbuf_tensor(f"wqbuf{i}", [128, 512], BF16)) for i in range(3)]
                P2.stage = [ph.enter_context(nc.sbuf_tensor(f"qkstage{i}", [128, 512], BF16)) for i in range(4)]
                P2.ps = [ph.enter_context(nc.psum_tensor(f"psp{i}", [128, 512], F32)) for i in range(8)]
                block = ph.enter_context(nc.Block("ph2"))
                run_phase(block, [("scalar", ph2_scalar),
                                  ("tensor", ph2_pe), ("vector", ph2_dve),
                                  ("sync", ph2_sync)])

            with ExitStack() as ph:
                P3.wtbuf = [ph.enter_context(nc.sbuf_tensor(f"wtbuf{i}", [128, 512], F32)) for i in range(3)]
                P3.wdq = [ph.enter_context(nc.sbuf_tensor(f"wdq{i}", [128, 512], BF16)) for i in range(2)]
                P3.g_stage = [ph.enter_context(nc.sbuf_tensor(f"gstage{i}", [128, 512], BF16)) for i in range(4)]
                P3.sc_all = ph.enter_context(nc.sbuf_tensor("sc_all", [128, 1024], F32))
                P3.ps = [ph.enter_context(nc.psum_tensor(f"psg{i}", [128, 512], F32)) for i in range(8)]
                block = ph.enter_context(nc.Block("ph3"))
                run_phase(block, [("sync", ph3_sync),
                                  ("vector", ph3_dve), ("tensor", ph3_pe),
                                  ("scalar", ph3_act), ("gpsimd", ph3_gp)])

        with ExitStack() as mid:
            A.ctxT = mid.enter_context(nc.sbuf_tensor("ctxT", [128, H * T], BF16))

            with ExitStack() as ph:
                P4.kbuf = [ph.enter_context(nc.sbuf_tensor(f"kbuf{i}", [128, S], BF16)) for i in range(2)]
                P4.vbuf = [ph.enter_context(nc.sbuf_tensor(f"vbuf{i}", [128, S], BF16)) for i in range(2)]
                P4.qbuf = [ph.enter_context(nc.sbuf_tensor(f"qbuf{i}", [128, T], BF16)) for i in range(2)]
                P4.pbuf = [ph.enter_context(nc.sbuf_tensor(f"pbuf{i}", [128, T], BF16)) for i in range(6)]
                P4.R = ph.enter_context(nc.sbuf_tensor("Rsum", [128, T], F32))
                P4.lntmp = ph.enter_context(nc.sbuf_tensor("lntmp", [1, T], F32))
                P4.rl_row = ph.enter_context(nc.sbuf_tensor("rl_row", [1, T], F32))
                P4.rlb_sb = ph.enter_context(nc.sbuf_tensor("rlb_sb", [128, T], F32))
                P4.ps_s = [ph.enter_context(nc.psum_tensor(f"ps_s{i}", [128, 1024], F32)) for i in range(2)]
                P4.ps_c = ph.enter_context(nc.psum_tensor("ps_c", [128, 1024], F32))
                block = ph.enter_context(nc.Block("ph4"))
                run_phase(block, [("sync", ph4_sync), ("tensor", ph4_pe),
                                  ("scalar", ph4_act), ("gpsimd", ph4_gp),
                                  ("vector", ph4_dve)])

            with ExitStack() as ph:
                P5.wstg = [ph.enter_context(nc.sbuf_tensor(f"wstg5_{i}", [128, 512], F32)) for i in range(3)]
                P5.owbuf = [ph.enter_context(nc.sbuf_tensor(f"owbuf{i}", [128, 512], BF16)) for i in range(3)]
                P5.xtile = [ph.enter_context(nc.sbuf_tensor(f"xtile{i}", [128, 512], F32)) for i in range(3)]
                P5.gtile = [ph.enter_context(nc.sbuf_tensor(f"gtile{i}", [128, 512], BF16)) for i in range(3)]
                P5.u = [ph.enter_context(nc.sbuf_tensor(f"u{i}", [128, 512], F32)) for i in range(2)]
                P5.ostage = [ph.enter_context(nc.sbuf_tensor(f"ostage{i}", [128, 512], F32)) for i in range(3)]
                P5.ps = [ph.enter_context(nc.psum_tensor(f"pso{i}", [128, 512], F32)) for i in range(8)]
                block = ph.enter_context(nc.Block("ph5"))
                run_phase(block, [("scalar", ph5_scalar), ("tensor", ph5_pe),
                                  ("sync", ph5_sync), ("vector", ph5_dve)])

    return nc


def kernel(L0_attn_x, L0_attn_attn_qkvw, L0_attn_attn_qkvb, L0_attn_attn_ow,
           L0_attn_attn_ob, L0_ffn_weight, L0_ffn_weight_scale):
    x = np.ascontiguousarray(np.asarray(L0_attn_x, dtype=np.float32).reshape(S, D))
    wqkv = np.ascontiguousarray(np.asarray(L0_attn_attn_qkvw, dtype=np.float32))
    bqkv = np.ascontiguousarray(np.asarray(L0_attn_attn_qkvb, dtype=np.float32))
    wo = np.ascontiguousarray(np.asarray(L0_attn_attn_ow, dtype=np.float32))
    ob = np.ascontiguousarray(np.asarray(L0_attn_attn_ob, dtype=np.float32))
    wt = np.ascontiguousarray(np.asarray(L0_ffn_weight, dtype=np.float32).T)
    scale = np.asarray(L0_ffn_weight_scale, dtype=np.float32)
    screp = np.ascontiguousarray(
        np.broadcast_to(scale.reshape(1, 32 * 32), (128, 32 * 32)))

    nc = build()
    in_maps = []
    for c in range(NCORES):
        in_maps.append({
            "x": x[T * c:T * (c + 1)],
            "wqkv": wqkv, "bqkv": bqkv, "wo": wo, "ob": ob,
            "wt": wt, "screp": screp,
        })
    res = run_bass_kernel_spmd(nc, in_maps, core_ids=list(range(NCORES)))
    out = np.concatenate([res.results[c]["y"] for c in range(NCORES)], axis=0)
    return out.reshape(1, S, D)


# version 16
# speedup vs baseline: 1.0249x; 1.0249x over previous
import math
from contextlib import ExitStack

import numpy as np

import concourse.bass as bass
import concourse.mybir as mybir
from concourse.bass_utils import run_bass_kernel_spmd

F32 = mybir.dt.float32
BF16 = mybir.dt.bfloat16
AF = mybir.ActivationFunctionType
ALU = mybir.AluOpType
AX = mybir.AxisListType

NCORES = 8
T = 1024
D = 4096
H = 32
S = 8192
FO = 3 * D
NT = T // 128
NF = D // 128
SM_SCALE = 1.0 / math.sqrt(128.0)

MACS_K = list(range(8, 16))
MACS_V = list(range(16, 24))
MACS_Q = list(range(0, 8))
MAC_ORDER = MACS_Q + MACS_K + MACS_V


class _FakeInstr:
    def then_inc(self, *a, **k):
        return self


class _FakeEng:
    def __getattr__(self, name):
        return lambda *a, **k: _FakeInstr()


class Ctx:

    def __init__(self):
        self.mode = "dry"
        self.ev = {}
        self.marks = {}
        self.dry = {}
        self.emitc = {}

    def inc(self, instr, sem, k, key=None):
        cnt = self.dry if self.mode == "dry" else self.emitc
        name = sem[0]
        cnt[name] = cnt.get(name, 0) + k
        if self.mode == "dry":
            if key is not None:
                assert key not in self.ev, key
                self.ev[key] = (sem, cnt[name])
        else:
            if key is not None:
                s, v = self.ev[key]
                assert s[0] == name and v == cnt[name], (key, v, cnt[name])
            instr.then_inc(sem[1], k)
        return cnt[name]

    def wait(self, eng, key):
        if self.mode == "dry":
            return
        sem, val = self.ev[key]
        eng.wait_ge(sem[1], val)

    def wait_val(self, eng, sem, val):
        if self.mode == "dry":
            return
        eng.wait_ge(sem[1], val)

    def mark(self, key, sems):
        cnt = self.dry if self.mode == "dry" else self.emitc
        snap = [(s, cnt.get(s[0], 0)) for s in sems]
        if self.mode == "dry":
            self.marks[key] = snap
        else:
            assert self.marks[key] == snap, (key, self.marks[key], snap)

    def wait_mark(self, eng, key):
        if self.mode == "dry":
            return
        for sem, val in self.marks[key]:
            if val > 0:
                eng.wait_ge(sem[1], val)


def build(dbg=False):
    nc = bass.Bass("TRN2", target_bir_lowering=False)

    x_in = nc.dram_tensor("x", [T, D], F32, kind="ExternalInput").ap()
    wqkv = nc.dram_tensor("wqkv", [D, FO], F32, kind="ExternalInput").ap()
    bqkv = nc.dram_tensor("bqkv", [FO], F32, kind="ExternalInput").ap()
    wo = nc.dram_tensor("wo", [D, D], F32, kind="ExternalInput").ap()
    ob_in = nc.dram_tensor("ob", [D], F32, kind="ExternalInput").ap()
    wt_in = nc.dram_tensor("wt", [D, D], F32, kind="ExternalInput").ap()
    screp = nc.dram_tensor("screp", [128, 1024], F32, kind="ExternalInput").ap()

    y_out = nc.dram_tensor("y", [T, D], F32, kind="ExternalOutput").ap()

    k_src = nc.dram_tensor("k_src", [D, T], BF16, kind="Internal").ap()
    v_src = nc.dram_tensor("v_src", [D, T], BF16, kind="Internal").ap()
    gath_k = [nc.dram_tensor(f"gath_k{i}", [NCORES * D // 4, T], BF16, kind="Internal").ap() for i in range(4)]
    gath_v = [nc.dram_tensor(f"gath_v{i}", [NCORES * D // 4, T], BF16, kind="Internal").ap() for i in range(4)]
    q_dram = nc.dram_tensor("q_dram", [D, T], BF16, kind="Internal").ap()
    g_dram = nc.dram_tensor("g_dram", [T, D], BF16, kind="Internal").ap()

    v_src_4d = v_src.rearrange("(hh p) (tt dd) -> p hh tt dd", p=128, dd=128)

    C = Ctx()

    with ExitStack() as top:
        def sem(name):
            return (name, top.enter_context(nc.semaphore(name)))

        def sems(name, n):
            return [sem(f"{name}{i}") for i in range(n)]

        s_pe = sem("s_pe")
        s_ac = sem("s_ac")
        s_dv = sem("s_dv")
        s_gc = sem("s_gc")
        s_colk = sems("s_colk", 4)
        s_colv = sems("s_colv", 4)
        s_x = sems("s_x", 2)
        s_bias = sem("s_bias")
        s_scl = sem("s_scl")
        s_wq = sems("s_wq", 3)
        s_st = sems("s_st", 4)
        s_wt = sems("s_wt", 3)
        s_gs = sems("s_gs", 4)
        s_kv = sems("s_kv", 2)
        s_wo = sems("s_wo", 3)
        s_cx = sems("s_cx", 3)
        s_y = sems("s_y", 3)

        ones_bf = top.enter_context(nc.sbuf_tensor("ones_bf", [1, 512], BF16))
        ones_f = top.enter_context(nc.sbuf_tensor("ones_f", [128, 1], F32))
        ones_fr = top.enter_context(nc.sbuf_tensor("ones_fr", [1, 128], F32))
        epst = top.enter_context(nc.sbuf_tensor("epst", [128, 1], F32))
        ident = top.enter_context(nc.sbuf_tensor("ident", [128, 128], BF16))
        ob_bf = top.enter_context(nc.sbuf_tensor("ob_bf", [1, D], BF16))

        def ph1_sync(e):
            for tt in range(NT):
                if tt >= 2:
                    C.wait(e, ("xn", tt - 2))
                C.inc(e.dma_start(out=P1.xbuf[tt % 2][:],
                                  in_=x_in[128 * tt:128 * (tt + 1), :]),
                      s_x[tt % 2], 16, ("xld", tt))

        def ph1_gp(e):
            C.inc(e.memset(ident[:], 0.0), s_gc, 1, ("identm",))
            C.wait(e, ("identm",))
            C.inc(e.affine_select(out=ident[:], in_=ident[:],
                                  compare_op=ALU.not_equal, fill=1.0, base=0,
                                  pattern=[[-1, 128]], channel_multiplier=1),
                  s_gc, 1, ("ident",))
            C.inc(e.dma_start(out=A.bq_bf[:], in_=bqkv[None, :]), s_bias, 16)
            C.inc(e.dma_start(out=ob_bf[:], in_=ob_in[None, :]), s_bias, 16,
                  ("biasld",))

        def ph1_act(e):
            for tt in range(NT):
                C.wait(e, ("xld", tt))
                C.inc(e.activation(P1.sq_dump[:], P1.xbuf[tt % 2][:], AF.Square,
                                   accum_out=P1.ssq[tt % 2][:]),
                      s_ac, 1, ("ssq", tt))
                C.wait(e, ("var", tt))
                C.inc(e.activation(P1.lnv[tt % 2][:], P1.varb[tt % 2][:], AF.Ln,
                                   bias=epst[:]), s_ac, 1, ("lnv", tt))
                C.wait(e, ("lnv", tt))
                C.inc(e.activation(P1.rstd[tt % 2][:], P1.lnv[tt % 2][:], AF.Exp,
                                   scale=-0.5), s_ac, 1, ("rstd", tt))

        def ph1_dve(e):
            C.inc(e.memset(ones_bf[:], 1.0), s_dv, 1)
            C.inc(e.memset(ones_f[:], 1.0), s_dv, 1)
            C.inc(e.memset(ones_fr[:], 1.0), s_dv, 1)
            C.inc(e.memset(epst[:], 1e-5), s_dv, 1, ("consts",))
            for tt in range(NT):
                C.wait(e, ("xld", tt))
                C.inc(e.reduce_sum(P1.srow[tt % 2][:], P1.xbuf[tt % 2][:], axis=AX.X),
                      s_dv, 1, ("srow", tt))
                C.wait(e, ("srow", tt))
                C.inc(e.tensor_scalar_mul(P1.mean[tt % 2][:], P1.srow[tt % 2][:], 1.0 / D),
                      s_dv, 1, ("mean", tt))
                C.wait(e, ("ssq", tt))
                C.inc(e.tensor_scalar_mul(P1.vtmp[tt % 2][:], P1.ssq[tt % 2][:], 1.0 / D),
                      s_dv, 1, ("vtmp", tt))
                C.wait(e, ("mean", tt))
                C.inc(e.tensor_tensor(P1.m2[tt % 2][:], P1.mean[tt % 2][:],
                                      P1.mean[tt % 2][:], ALU.mult),
                      s_dv, 1, ("m2", tt))
                C.wait(e, ("m2", tt))
                C.inc(e.tensor_tensor(P1.varb[tt % 2][:], P1.vtmp[tt % 2][:],
                                      P1.m2[tt % 2][:], ALU.subtract),
                      s_dv, 1, ("var", tt))
                C.wait(e, ("rstd", tt))
                if tt >= 2:
                    C.wait(e, ("tr", tt - 2, NF - 1))
                C.inc(e.tensor_scalar(out=P1.xn_t[tt % 2][:], in0=P1.xbuf[tt % 2][:],
                                      scalar1=P1.mean[tt % 2][:],
                                      scalar2=P1.rstd[tt % 2][:],
                                      op0=ALU.subtract, op1=ALU.mult),
                      s_dv, 1, ("xn", tt))
                for fb in range(NF):
                    C.wait(e, ("tr", tt, fb))
                    C.inc(e.tensor_copy(A.xnT[:, fb * T + tt * 128: fb * T + (tt + 1) * 128],
                                        P1.pst[fb % 2][:]),
                          s_dv, 1, ("tre", tt, fb))

        def ph1_pe(e):
            for tt in range(NT):
                C.wait(e, ("xn", tt))
                if tt == 0:
                    C.wait(e, ("ident",))
                for fb in range(NF):
                    if tt > 0 or fb >= 2:
                        pi = tt * NF + fb - 2
                        C.wait(e, ("tre", pi // NF, pi % NF))
                    C.inc(e.transpose(P1.pst[fb % 2][:],
                                      P1.xn_t[tt % 2][:, 128 * fb:128 * (fb + 1)],
                                      ident[:]),
                          s_pe, 1, ("tr", tt, fb))

        def qkv_dst(m, j, th):
            if m >= 8:
                r0 = 512 * (m - 8) + 128 * j
                return k_src[r0:r0 + 128, 512 * th:512 * (th + 1)]
            r0 = 512 * m + 128 * j
            return q_dram[r0:r0 + 128, 512 * th:512 * (th + 1)]

        def ph3_gp(e):
            for q in range(4):
                if q == 0:
                    C.wait(e, ("mmf", 5 * NF - 1))
                elif q == 1:
                    C.wait(e, ("mmf", 8 * NF - 1))
                C.wait_mark(e, f"k_st{q}")
                C.inc(e.collective_compute(
                    "AllGather", ALU.bypass,
                    replica_groups=[list(range(NCORES))],
                    ins=[k_src[1024 * q:1024 * (q + 1), :].opt()],
                    outs=[gath_k[q].opt()]),
                    s_colk[q], 1, ("colk", q))
                C.wait_mark(e, f"v_st{q}")
                C.inc(e.collective_compute(
                    "AllGather", ALU.bypass,
                    replica_groups=[list(range(NCORES))],
                    ins=[v_src[1024 * q:1024 * (q + 1), :].opt()],
                    outs=[gath_v[q].opt()]),
                    s_colv[q], 1, ("colv", q))

        def ph2_scalar(e):
            C.wait(e, ("tre", NT - 1, NF - 1))
            for mi, m in enumerate(MAC_ORDER):
                for fb in range(NF):
                    n = mi * NF + fb
                    if n >= 3:
                        C.wait(e, ("wqc", n - 3))
                    C.inc(e.dma_start(out=P2.wstg[n % 3][:],
                                      in_=wqkv[128 * fb:128 * (fb + 1),
                                               512 * m:512 * (m + 1)]),
                          s_wq[n % 3], 16, ("wq", n))

        def ph2_pe(e):
            for mi, m in enumerate(MAC_ORDER):
                std = m < 16
                for fb in range(NF):
                    n = mi * NF + fb
                    C.wait(e, ("wqc", n))
                    if fb == 0:
                        if mi == 0:
                            C.wait(e, ("consts",))
                            C.wait(e, ("biasld",))
                    for k in range(8):
                        if fb == 0 and mi >= 1:
                            C.wait(e, ("evq", mi - 1, k))
                        if std:
                            j, th = k // 2, k % 2
                            ins = e.matmul(P2.ps[k][:],
                                           P2.wqbuf[n % 3][:, 128 * j:128 * (j + 1)],
                                           A.xnT[:, fb * T + 512 * th: fb * T + 512 * (th + 1)],
                                           start=(fb == 0), stop=False)
                        else:
                            tt = k
                            ins = e.matmul(P2.ps[k][:],
                                           A.xnT[:, fb * T + 128 * tt: fb * T + 128 * (tt + 1)],
                                           P2.wqbuf[n % 3][:, 0:512],
                                           start=(fb == 0), stop=False)
                        if k == 7:
                            C.inc(ins, s_pe, 1, ("mmq", n))
                for k in range(8):
                    if std:
                        j, th = k // 2, k % 2
                        ins = e.matmul(P2.ps[k][:],
                                       A.bq_bf[0:1, 512 * m + 128 * j: 512 * m + 128 * (j + 1)],
                                       ones_bf[0:1, 0:512], start=False, stop=True)
                    else:
                        ins = e.matmul(P2.ps[k][:],
                                       ones_bf[0:1, 0:128],
                                       A.bq_bf[0:1, 512 * m:512 * (m + 1)],
                                       start=False, stop=True)
                    if k == 7:
                        C.inc(ins, s_pe, 1, ("pst", mi))

        def ph2_dve(e):
            def cast(n):
                C.wait(e, ("wq", n))
                if n >= 3:
                    C.wait(e, ("mmq", n - 3))
                C.inc(e.tensor_copy(P2.wqbuf[n % 3][:], P2.wstg[n % 3][:]),
                      s_dv, 1, ("wqc", n))

            def evacs(mi):
                C.wait(e, ("pst", mi))
                for k in range(8):
                    si = mi * 8 + k
                    if si >= 4:
                        C.wait(e, ("kvst", (si - 4) // 8, (si - 4) % 8))
                    C.inc(e.tensor_copy(P2.stage[si % 4][:], P2.ps[k][:]),
                          s_dv, 1, ("evq", mi, k))

            for mi in range(len(MAC_ORDER)):
                for fb in range(3):
                    cast(mi * NF + fb)
                if mi >= 1:
                    evacs(mi - 1)
                for fb in range(3, NF):
                    cast(mi * NF + fb)
            evacs(len(MAC_ORDER) - 1)

        def ph2_sync(e):
            for mi, m in enumerate(MAC_ORDER):
                for k in range(8):
                    si = mi * 8 + k
                    C.wait(e, ("evq", mi, k))
                    if m < 16:
                        j, th = k // 2, k % 2
                        dst = qkv_dst(m, j, th)
                        src = P2.stage[si % 4][:]
                    else:
                        tt = k
                        h0 = 4 * (m - 16)
                        dst = v_src_4d[:, h0:h0 + 4, tt, :]
                        src = P2.stage[si % 4][:].rearrange("p (hs dd) -> p hs dd", dd=128)
                    C.inc(e.dma_start(out=dst, in_=src), s_st[si % 4], 16,
                          ("kvst", mi, k))
                if mi == 7:
                    C.mark("q_stored", s_st)
                if mi >= 9 and mi <= 15 and mi % 2 == 1:
                    C.mark(f"k_st{(mi - 9) // 2}", s_st)
                if mi >= 17 and mi % 2 == 1:
                    C.mark(f"v_st{(mi - 17) // 2}", s_st)

        def ph3_sync(e):
            C.wait_mark(e, "v_st3")
            C.inc(e.dma_start(out=P3.sc_all[:], in_=screp), s_scl, 16, ("scld",))
            for oc in range(8):
                for ib in range(NF):
                    n = oc * NF + ib
                    if n >= 3:
                        C.wait(e, ("dq", n - 3))
                    C.inc(e.dma_start(out=P3.wtbuf[n % 3][:],
                                      in_=wt_in[128 * ib:128 * (ib + 1),
                                                512 * oc:512 * (oc + 1)]),
                          s_wt[n % 3], 16, ("wt", n))
                for tt in range(NT):
                    gi = oc * NT + tt
                    C.wait(e, ("sil", oc, tt))
                    C.inc(e.dma_start(out=g_dram[128 * tt:128 * (tt + 1),
                                                 512 * oc:512 * (oc + 1)],
                                      in_=P3.g_stage[gi % 4][:]),
                          s_gs[gi % 4], 16, ("gst", oc, tt))
                if oc == 7:
                    C.mark("g_stored", s_gs)

        def ph3_dve(e):
            for oc in range(8):
                for ib in range(NF):
                    n = oc * NF + ib
                    C.wait(e, ("wt", n))
                    if n == 0:
                        C.wait(e, ("scld",))
                    if n >= 2:
                        C.wait(e, ("mmf", n - 2))
                    for c in range(4):
                        o_blk = 4 * oc + c
                        ins = e.tensor_scalar_mul(
                            P3.wdq[n % 2][:, 128 * c:128 * (c + 1)],
                            P3.wtbuf[n % 3][:, 128 * c:128 * (c + 1)],
                            P3.sc_all[:, 32 * o_blk + ib:32 * o_blk + ib + 1])
                        if c == 3:
                            C.inc(ins, s_dv, 1, ("dq", n))

        def ph3_pe(e):
            for oc in range(8):
                for ib in range(NF):
                    n = oc * NF + ib
                    C.wait(e, ("dq", n))
                    for tt in range(NT):
                        if ib == 0 and oc >= 1:
                            C.wait(e, ("sil", oc - 1, tt))
                        ins = e.matmul(P3.ps[tt][:],
                                       A.xnT[:, ib * T + 128 * tt: ib * T + 128 * (tt + 1)],
                                       P3.wdq[n % 2][:, 0:512],
                                       start=(ib == 0), stop=(ib == NF - 1))
                        if tt == NT - 1:
                            C.inc(ins, s_pe, 1, ("mmf", n))

        def ph3_act(e):
            for oc in range(8):
                C.wait(e, ("mmf", oc * NF + NF - 1))
                for tt in range(NT):
                    gi = oc * NT + tt
                    if gi >= 4:
                        C.wait(e, ("gst", (gi - 4) // NT, (gi - 4) % NT))
                    C.inc(e.activation(P3.g_stage[gi % 4][:], P3.ps[tt][:], AF.Silu),
                          s_ac, 1, ("sil", oc, tt))

        def ph4_sync(e):
            for h in range(H):
                if h == 0:
                    C.wait_mark(e, "v_st3")
                    C.wait_mark(e, "g_stored")
                    C.wait(e, ("mmf", 8 * NF - 1))
                    C.wait(e, ("sil", 7, NT - 1))
                if h % 8 == 0:
                    C.wait_val(e, s_colk[h // 8], 1)
                    C.wait_val(e, s_colv[h // 8], 1)
                if h >= 2:
                    C.wait(e, ("ctxe", h - 2))
                qq, hh = h // 8, h % 8
                for r in range(NCORES):
                    C.inc(e.dma_start(
                        out=P4.kbuf[h % 2][:, T * r:T * (r + 1)],
                        in_=gath_k[qq][1024 * r + 128 * hh: 1024 * r + 128 * (hh + 1), :]),
                        s_kv[h % 2], 16)
                for r in range(NCORES):
                    C.inc(e.dma_start(
                        out=P4.vbuf[h % 2][:, T * r:T * (r + 1)],
                        in_=gath_v[qq][1024 * r + 128 * hh: 1024 * r + 128 * (hh + 1), :]),
                        s_kv[h % 2], 16)
                C.inc(e.dma_start(out=P4.qbuf[h % 2][:],
                                  in_=q_dram[128 * h:128 * (h + 1), :]),
                      s_kv[h % 2], 16, ("kvld", h))

        def ph4_pe(e):
            for h in range(H):
                C.wait(e, ("kvld", h))

                def smm(kc):
                    g = h * 64 + kc
                    if g >= 2:
                        C.wait(e, ("exp", g - 2))
                    if kc == 0 and h >= 1:
                        C.wait(e, ("rcp", h - 1))
                    if kc == 1 and h >= 1:
                        C.wait(e, ("rlbc", h - 1))
                    for th in range(2):
                        ins = e.matmul(P4.ps_s[kc % 2][:, 512 * th:512 * (th + 1)],
                                       P4.kbuf[h % 2][:, 128 * kc:128 * (kc + 1)],
                                       P4.qbuf[h % 2][:, 512 * th:512 * (th + 1)],
                                       start=True, stop=True)
                        if th == 1:
                            C.inc(ins, s_pe, 1, ("smm", g))

                def cmm(kc):
                    g = h * 64 + kc
                    C.wait(e, ("exp", g))
                    if kc == 0 and h >= 1:
                        C.wait(e, ("ctxe", h - 1))
                    for th in range(2):
                        ins = e.matmul(P4.ps_c[:, 512 * th:512 * (th + 1)],
                                       P4.vbuf[h % 2][:, 128 * kc:128 * (kc + 1)],
                                       P4.pbuf[g % NPB][:, 512 * th:512 * (th + 1)],
                                       start=(kc == 0), stop=(kc == 63))
                        if th == 1:
                            C.inc(ins, s_pe, 1, ("cmm", g))

                for kc in range(64):
                    smm(kc)
                    if kc >= 1:
                        cmm(kc - 1)
                cmm(63)
                C.wait(e, ("radd", h * 64 + 63))
                C.wait(e, ("radg", h * 64 + 63))
                for th in range(2):
                    ins = e.matmul(P4.ps_s[0][0:1, 512 * th:512 * (th + 1)],
                                   ones_f[:], P4.R[:, 512 * th:512 * (th + 1)],
                                   start=True, stop=True)
                    if th == 1:
                        C.inc(ins, s_pe, 1, ("lmm", h))
                C.wait(e, ("rcp", h))
                for th in range(2):
                    ins = e.matmul(P4.ps_s[1][:, 512 * th:512 * (th + 1)],
                                   ones_fr[:], P4.rl_row[0:1, 512 * th:512 * (th + 1)],
                                   start=True, stop=True)
                    if th == 1:
                        C.inc(ins, s_pe, 1, ("rlb", h))

        DV_W = 736
        NPB = 6

        def ph4_act(e):
            for h in range(H):
                for kc in range(64):
                    g = h * 64 + kc
                    C.wait(e, ("smm", g))
                    if g >= NPB:
                        C.wait(e, ("cmm", g - NPB))
                        C.wait(e, ("radd", g - NPB))
                        C.wait(e, ("radg", g - NPB))
                    C.inc(e.activation(P4.pbuf[g % NPB][:], P4.ps_s[kc % 2][:],
                                       AF.Exp, scale=SM_SCALE),
                          s_ac, 1, ("exp", g))
                C.wait(e, ("lmm", h))
                if h >= 1:
                    C.wait(e, ("rlb", h - 1))
                C.inc(e.activation(P4.lntmp[:], P4.ps_s[0][0:1, 0:1024], AF.Ln),
                      s_ac, 1, ("lnl", h))
                C.wait(e, ("lnl", h))
                C.inc(e.activation(P4.rl_row[:], P4.lntmp[:], AF.Exp, scale=-1.0),
                      s_ac, 1, ("rcp", h))

        def ph4_gp(e):
            for h in range(H):
                for kc in range(64):
                    g = h * 64 + kc
                    C.wait(e, ("exp", g))
                    if kc == 0:
                        if h >= 1:
                            C.wait(e, ("lmm", h - 1))
                        ins = e.tensor_copy(P4.R[:, DV_W:T], P4.pbuf[g % NPB][:, DV_W:T])
                    else:
                        C.wait(e, ("radg", g - 1))
                        ins = e.tensor_tensor(P4.R[:, DV_W:T], P4.R[:, DV_W:T],
                                              P4.pbuf[g % NPB][:, DV_W:T], ALU.add)
                    C.inc(ins, s_gc, 1, ("radg", g))

        def ph4_dve(e):
            for h in range(H):
                for kc in range(64):
                    g = h * 64 + kc
                    C.wait(e, ("exp", g))
                    if kc == 0:
                        ins = e.tensor_copy(P4.R[:, 0:DV_W], P4.pbuf[g % NPB][:, 0:DV_W])
                    else:
                        C.wait(e, ("radd", g - 1))
                        ins = e.tensor_tensor(P4.R[:, 0:DV_W], P4.R[:, 0:DV_W],
                                              P4.pbuf[g % NPB][:, 0:DV_W], ALU.add)
                    C.inc(ins, s_dv, 1, ("radd", g))
                C.wait(e, ("rlb", h))
                C.inc(e.tensor_copy(P4.rlb_sb[:], P4.ps_s[1][:]), s_dv, 1, ("rlbc", h))
                C.wait(e, ("rlbc", h))
                C.wait(e, ("cmm", h * 64 + 63))
                C.inc(e.tensor_tensor(A.ctxT[:, T * h:T * (h + 1)], P4.ps_c[:],
                                      P4.rlb_sb[:], ALU.mult),
                      s_dv, 1, ("ctxe", h))

        def ph5_scalar(e):
            C.wait(e, ("ctxe", H - 1))
            for oc in range(8):
                for fb in range(NF):
                    n = oc * NF + fb
                    if n >= 3:
                        C.wait(e, ("woc", n - 3))
                    C.inc(e.dma_start(out=P5.wstg[n % 3][:],
                                      in_=wo[128 * fb:128 * (fb + 1),
                                             512 * oc:512 * (oc + 1)]),
                          s_wo[n % 3], 16, ("wo", n))

        def ph5_pe(e):
            for oc in range(8):
                for fb in range(NF):
                    n = oc * NF + fb
                    C.wait(e, ("woc", n))
                    for qt in range(NT):
                        if fb == 0 and oc >= 1:
                            C.wait(e, ("cmb", (oc - 1) * 8 + qt))
                        ins = e.matmul(P5.ps[qt][:],
                                       A.ctxT[:, fb * T + 128 * qt: fb * T + 128 * (qt + 1)],
                                       P5.owbuf[n % 3][:, 0:512],
                                       start=(fb == 0), stop=False)
                        if qt == NT - 1:
                            C.inc(ins, s_pe, 1, ("mmo", n))
                for qt in range(NT):
                    ins = e.matmul(P5.ps[qt][:], ones_bf[0:1, 0:128],
                                   ob_bf[0:1, 512 * oc:512 * (oc + 1)],
                                   start=False, stop=True)
                    if qt == NT - 1:
                        C.inc(ins, s_pe, 1, ("obias", oc))

        def ph5_sync(e):
            def load(ci):
                oc, qt = ci // 8, ci % 8
                C.inc(e.dma_start(out=P5.xtile[ci % 3][:],
                                  in_=x_in[128 * qt:128 * (qt + 1),
                                           512 * oc:512 * (oc + 1)]),
                      s_cx[ci % 3], 16)
                C.inc(e.dma_start(out=P5.gtile[ci % 3][:],
                                  in_=g_dram[128 * qt:128 * (qt + 1),
                                             512 * oc:512 * (oc + 1)]),
                      s_cx[ci % 3], 16, ("cg", ci))

            C.wait(e, ("ctxe", H - 1))
            for ci in range(3):
                load(ci)
            for ci in range(64):
                oc, qt = ci // 8, ci % 8
                C.wait(e, ("cmb", ci))
                C.inc(e.dma_start(out=y_out[128 * qt:128 * (qt + 1),
                                            512 * oc:512 * (oc + 1)],
                                  in_=P5.ostage[ci % 3][:]),
                      s_y[ci % 3], 16, ("yst", ci))
                if ci + 3 < 64:
                    load(ci + 3)
            C.mark("y_done", s_y)
            C.wait_mark(e, "y_done")

        def ph5_dve(e):
            def cast(n):
                C.wait(e, ("wo", n))
                if n >= 3:
                    C.wait(e, ("mmo", n - 3))
                C.inc(e.tensor_copy(P5.owbuf[n % 3][:], P5.wstg[n % 3][:]),
                      s_dv, 1, ("woc", n))

            def combine(ci):
                oc, qt = ci // 8, ci % 8
                C.wait(e, ("obias", oc))
                C.wait(e, ("cg", ci))
                if ci >= 2:
                    C.wait(e, ("cmb", ci - 2))
                C.inc(e.tensor_tensor(P5.u[ci % 2][:], P5.ps[qt][:],
                                      P5.gtile[ci % 3][:], ALU.mult),
                      s_dv, 1, ("umul", ci))
                C.wait(e, ("umul", ci))
                if ci >= 3:
                    C.wait(e, ("yst", ci - 3))
                C.inc(e.tensor_tensor(P5.ostage[ci % 3][:], P5.u[ci % 2][:],
                                      P5.xtile[ci % 3][:], ALU.add),
                      s_dv, 1, ("cmb", ci))

            for oc in range(8):
                for fb in range(3):
                    cast(oc * NF + fb)
                if oc >= 1:
                    for k in range(8):
                        combine((oc - 1) * 8 + k)
                for fb in range(3, NF):
                    cast(oc * NF + fb)
            for k in range(8):
                combine(7 * 8 + k)

        class NS:
            pass

        A = NS()
        P1 = NS()
        P2 = NS()
        P3 = NS()
        P4 = NS()
        P5 = NS()

        def run_phase(block, emitters):
            C.mode = "dry"
            for _, fn in emitters:
                fn(_FakeEng())
            C.mode = "emit"
            for sec, fn in emitters:
                getattr(block, sec)(lambda e, fn=fn: fn(e))

        with ExitStack() as mid:
            A.xnT = mid.enter_context(nc.sbuf_tensor("xnT", [128, NF * T], BF16))
            A.bq_bf = mid.enter_context(nc.sbuf_tensor("bq_bf", [1, FO], BF16))

            with ExitStack() as ph:
                P1.xbuf = [ph.enter_context(nc.sbuf_tensor(f"xbuf{i}", [128, D], F32)) for i in range(2)]
                P1.xn_t = [ph.enter_context(nc.sbuf_tensor(f"xn_t{i}", [128, D], BF16)) for i in range(2)]
                P1.sq_dump = ph.enter_context(nc.sbuf_tensor("sq_dump", [128, D], BF16))
                for nm in ["srow", "ssq", "mean", "vtmp", "m2", "varb", "lnv", "rstd"]:
                    setattr(P1, nm, [ph.enter_context(nc.sbuf_tensor(f"{nm}{i}", [128, 1], F32)) for i in range(2)])
                P1.pst = [ph.enter_context(nc.psum_tensor(f"pst{i}", [128, 128], BF16)) for i in range(2)]
                block = ph.enter_context(nc.Block("ph1"))
                run_phase(block, [("sync", ph1_sync), ("gpsimd", ph1_gp),
                                  ("scalar", ph1_act), ("vector", ph1_dve),
                                  ("tensor", ph1_pe)])

            with ExitStack() as ph:
                P2.wstg = [ph.enter_context(nc.sbuf_tensor(f"wstg{i}", [128, 512], F32)) for i in range(3)]
                P2.wqbuf = [ph.enter_context(nc.sbuf_tensor(f"wqbuf{i}", [128, 512], BF16)) for i in range(3)]
                P2.stage = [ph.enter_context(nc.sbuf_tensor(f"qkstage{i}", [128, 512], BF16)) for i in range(4)]
                P2.ps = [ph.enter_context(nc.psum_tensor(f"psp{i}", [128, 512], F32)) for i in range(8)]
                block = ph.enter_context(nc.Block("ph2"))
                run_phase(block, [("scalar", ph2_scalar),
                                  ("tensor", ph2_pe), ("vector", ph2_dve),
                                  ("sync", ph2_sync)])

            with ExitStack() as ph:
                P3.wtbuf = [ph.enter_context(nc.sbuf_tensor(f"wtbuf{i}", [128, 512], F32)) for i in range(3)]
                P3.wdq = [ph.enter_context(nc.sbuf_tensor(f"wdq{i}", [128, 512], BF16)) for i in range(2)]
                P3.g_stage = [ph.enter_context(nc.sbuf_tensor(f"gstage{i}", [128, 512], BF16)) for i in range(4)]
                P3.sc_all = ph.enter_context(nc.sbuf_tensor("sc_all", [128, 1024], F32))
                P3.ps = [ph.enter_context(nc.psum_tensor(f"psg{i}", [128, 512], F32)) for i in range(8)]
                block = ph.enter_context(nc.Block("ph3"))
                run_phase(block, [("sync", ph3_sync),
                                  ("vector", ph3_dve), ("tensor", ph3_pe),
                                  ("scalar", ph3_act), ("gpsimd", ph3_gp)])

        with ExitStack() as mid:
            A.ctxT = mid.enter_context(nc.sbuf_tensor("ctxT", [128, H * T], BF16))

            with ExitStack() as ph:
                P4.kbuf = [ph.enter_context(nc.sbuf_tensor(f"kbuf{i}", [128, S], BF16)) for i in range(2)]
                P4.vbuf = [ph.enter_context(nc.sbuf_tensor(f"vbuf{i}", [128, S], BF16)) for i in range(2)]
                P4.qbuf = [ph.enter_context(nc.sbuf_tensor(f"qbuf{i}", [128, T], BF16)) for i in range(2)]
                P4.pbuf = [ph.enter_context(nc.sbuf_tensor(f"pbuf{i}", [128, T], BF16)) for i in range(6)]
                P4.R = ph.enter_context(nc.sbuf_tensor("Rsum", [128, T], F32))
                P4.lntmp = ph.enter_context(nc.sbuf_tensor("lntmp", [1, T], F32))
                P4.rl_row = ph.enter_context(nc.sbuf_tensor("rl_row", [1, T], F32))
                P4.rlb_sb = ph.enter_context(nc.sbuf_tensor("rlb_sb", [128, T], F32))
                P4.ps_s = [ph.enter_context(nc.psum_tensor(f"ps_s{i}", [128, 1024], F32)) for i in range(2)]
                P4.ps_c = ph.enter_context(nc.psum_tensor("ps_c", [128, 1024], F32))
                block = ph.enter_context(nc.Block("ph4"))
                run_phase(block, [("sync", ph4_sync), ("tensor", ph4_pe),
                                  ("scalar", ph4_act), ("gpsimd", ph4_gp),
                                  ("vector", ph4_dve)])

            with ExitStack() as ph:
                P5.wstg = [ph.enter_context(nc.sbuf_tensor(f"wstg5_{i}", [128, 512], F32)) for i in range(3)]
                P5.owbuf = [ph.enter_context(nc.sbuf_tensor(f"owbuf{i}", [128, 512], BF16)) for i in range(3)]
                P5.xtile = [ph.enter_context(nc.sbuf_tensor(f"xtile{i}", [128, 512], F32)) for i in range(3)]
                P5.gtile = [ph.enter_context(nc.sbuf_tensor(f"gtile{i}", [128, 512], BF16)) for i in range(3)]
                P5.u = [ph.enter_context(nc.sbuf_tensor(f"u{i}", [128, 512], F32)) for i in range(2)]
                P5.ostage = [ph.enter_context(nc.sbuf_tensor(f"ostage{i}", [128, 512], F32)) for i in range(3)]
                P5.ps = [ph.enter_context(nc.psum_tensor(f"pso{i}", [128, 512], F32)) for i in range(8)]
                block = ph.enter_context(nc.Block("ph5"))
                run_phase(block, [("scalar", ph5_scalar), ("tensor", ph5_pe),
                                  ("sync", ph5_sync), ("vector", ph5_dve)])

    return nc


def kernel(L0_attn_x, L0_attn_attn_qkvw, L0_attn_attn_qkvb, L0_attn_attn_ow,
           L0_attn_attn_ob, L0_ffn_weight, L0_ffn_weight_scale):
    x = np.ascontiguousarray(np.asarray(L0_attn_x, dtype=np.float32).reshape(S, D))
    wqkv = np.ascontiguousarray(np.asarray(L0_attn_attn_qkvw, dtype=np.float32))
    bqkv = np.ascontiguousarray(np.asarray(L0_attn_attn_qkvb, dtype=np.float32))
    wo = np.ascontiguousarray(np.asarray(L0_attn_attn_ow, dtype=np.float32))
    ob = np.ascontiguousarray(np.asarray(L0_attn_attn_ob, dtype=np.float32))
    wt = np.ascontiguousarray(np.asarray(L0_ffn_weight, dtype=np.float32).T)
    scale = np.asarray(L0_ffn_weight_scale, dtype=np.float32)
    screp = np.ascontiguousarray(
        np.broadcast_to(scale.reshape(1, 32 * 32), (128, 32 * 32)))

    nc = build()
    in_maps = []
    for c in range(NCORES):
        in_maps.append({
            "x": x[T * c:T * (c + 1)],
            "wqkv": wqkv, "bqkv": bqkv, "wo": wo, "ob": ob,
            "wt": wt, "screp": screp,
        })
    res = run_bass_kernel_spmd(nc, in_maps, core_ids=list(range(NCORES)))
    out = np.concatenate([res.results[c]["y"] for c in range(NCORES)], axis=0)
    return out.reshape(1, S, D)
